# version 05
# baseline (speedup 1.0000x reference)
import numpy as np
import ml_dtypes
from contextlib import ExitStack

import concourse.bass as bass
import concourse.mybir as mybir
from concourse.bass_utils import run_bass_kernel_spmd

F32 = mybir.dt.float32
BF16 = mybir.dt.bfloat16
U8 = mybir.dt.uint8
AF = mybir.ActivationFunctionType
ALU = mybir.AluOpType
AX = mybir.AxisListType

D = 4096
T = 1024
NF = 32
NH = 32
NU = 16
INV_SQRT_DK = np.float32(1.0 / np.sqrt(128.0))
EXP_BIAS = np.float32(-3.0)
INV_SQRT_DK_FFN = float(1.0 / np.sqrt(32.0))
EPS = np.float32(1e-5)
INV_D = float(1.0 / 4096.0)

DMA_SEMS = ("Yc", "Yx0", "Yx1", "G", "Ykt0", "Ykt1", "Yvt0", "Yvt1",
            "Yxr0", "Yxr1", "Yxr2", "Yxr3",
            "Lk0", "Lk1", "Lv", "Lo0", "Lo1", "C")
HALVES = ((0, 512), (512, 1024))


class Sched:
    ENGINES = ("sync", "gpsimd", "scalar", "vector", "tensor")

    def __init__(self):
        self.ops = {e: [] for e in self.ENGINES}
        self.count = {}
        self.producer = {}

    def sem(self, name, producer):
        self.count[name] = 0
        self.producer[name] = producer

    def add(self, engine, emit, waits=(), inc=None, inc_by=1):
        if engine == "vector" and inc is None and emit is not None:
            inc = "V"
        tok = None
        if inc is not None:
            self.count[inc] += inc_by
            tok = (inc, self.count[inc])
        self.ops[engine].append((emit, [w for w in waits if w is not None], inc, inc_by))
        return tok

    def final(self, sem):
        return (sem, self.count[sem])


def build_nc():
    nc = bass.Bass(trn_type="TRN2", target_bir_lowering=False)

    xT = nc.dram_tensor("xT", [D, T], F32, kind="ExternalInput").ap()
    WQs = nc.dram_tensor("WQs", [NF, 128, D], F32, kind="ExternalInput").ap()
    WKs = nc.dram_tensor("WKs", [NF, 128, D], F32, kind="ExternalInput").ap()
    WVs = nc.dram_tensor("WVs", [NF, 128, D], F32, kind="ExternalInput").ap()
    WOs = nc.dram_tensor("WOs", [NF, 128, D], F32, kind="ExternalInput").ap()
    Ws = nc.dram_tensor("Ws", [NF, 128, D], F32, kind="ExternalInput").ap()
    QKAsd = nc.dram_tensor("QKAsd", [128, 20480], F32, kind="ExternalInput").ap()
    BmPd = nc.dram_tensor("BmPd", [128, D], F32, kind="ExternalInput").ap()
    bbd = nc.dram_tensor("bbd", [128, 32], F32, kind="ExternalInput").ap()
    identd = nc.dram_tensor("identd", [128, 128], BF16, kind="ExternalInput").ap()
    onesd = nc.dram_tensor("onesd", [128, 128], BF16, kind="ExternalInput").ap()
    ag_in2 = nc.dram_tensor("ag_in2", [4, 2048, T], BF16).ap()
    ag_out2 = nc.dram_tensor("ag_out2", [4, 4, 2048, T], BF16).ap()
    outT = nc.dram_tensor("outT", [D, T], F32, kind="ExternalOutput").ap()

    ctx = ExitStack()
    SBUF_TOTAL = 212832
    arena = ctx.enter_context(nc.sbuf_tensor("arena", [128, SBUF_TOTAL], U8))
    sb_base = nc.lookup_mloc(arena).addr

    def sb(name, shape, dtype, off):
        per_part = int(np.prod(shape[1:])) * mybir.dt.size(dtype)
        assert off % 32 == 0 and off + per_part <= SBUF_TOTAL, (name, off, per_part)
        h = nc.virtual_sbuf_tensor(name, list(shape), dtype)
        nc._tensor(h.name, h.shape, h.dtype, type="SB")
        ml = nc.lookup_mloc(h)
        ml.allocated = True
        ml.addr = sb_base + off
        return h

    C = 131072
    A = sb("A", [128, 32 * 1024], BF16, 0)
    B = sb("B", [128, 32 * 1024], BF16, 65536)
    wbf = [sb(f"wbf{i}", [128, 4096], BF16, C + i * 8192) for i in range(2)]
    xr = [sb(f"xr{i}", [128, 1024], F32, C + i * 4096) for i in range(4)]
    kt = [sb(f"kt{i}", [128, 4096], BF16, C + 16384 + i * 8192) for i in range(2)]
    vt = [sb(f"vt{i}", [128, 4096], BF16, C + 32768 + i * 8192) for i in range(2)]
    xstage = [sb(f"xstage{i}", [128, 4096], F32, C + 16384 + i * 16384) for i in range(2)]
    et = [sb(f"et{i}", [128, 1024], BF16, C + 49152 + i * 2048) for i in range(2)]
    invat = sb("invat", [128, 512], F32, C + 53248)
    sq1 = [sb(f"sq1_{i}", [128, 1024], BF16, C + 49152 + i * 2048) for i in range(2)]
    kev = [sb(f"kev{i}", [128, 1024], BF16, C + 55296 + i * 2048) for i in range(2)]
    vev = sb("vev", [128, 1024], BF16, C + 59392)
    vtr = sb("vtr", [128, 1024], BF16, C + 61440)
    qkabf = sb("qkabf", [128, 20480], BF16, C + 16384)
    pt0 = sb("pt0", [128, 1024], F32, C + 49152)
    vhalf = sb("vhalf", [128, 16384], BF16, C + 16384)
    BmP = sb("BmP", [128, 4096], BF16, C + 57344)
    sqo = [sb(f"sqo{i}", [128, 1024], BF16, C + 53248 + i * 2048) for i in range(2)]
    gt = sb("gt", [128, 1024], BF16, 196608)
    mstat = sb("mstat", [128, 1024], F32, 198656)
    invstat = sb("invstat", [128, 1024], F32, 202752)
    scr = sb("scr", [128, 1024], F32, 206848)
    kbuf = sb("kbuf", [128, 2048], BF16, 198656)
    gtile = sb("gtile", [128, 128], BF16, 202752 + 1024)
    hbuf = sb("hbuf", [128, 1024], BF16, 202752 + 2048)
    ident = sb("ident", [128, 128], BF16, 210944)
    ones = sb("ones", [128, 128], BF16, 211200)
    bb = sb("bb", [128, 32], F32, 211456)
    sc8 = sb("sc8", [128, 8], F32, 211584)
    smax = sb("smax", [128, 8], F32, 211616)
    ssum = sb("ssum", [128, 8], F32, 211648)
    w8 = sb("w8", [128, 8], F32, 211680)
    srec = sb("srec", [128, 8], F32, 211712)
    sq8 = sb("sq8", [128, 512], BF16, 211744)

    psarena = ctx.enter_context(nc.psum_tensor("psarena", [128, 4096], F32))
    ps_base = nc.lookup_mloc(psarena).addr

    def psum(name, bank, dtype=F32, width=1024):
        h = nc.virtual_psum_tensor(name, [128, width], dtype)
        nc._tensor(h.name, h.shape, h.dtype, type="PSUM")
        ml = nc.lookup_mloc(h)
        ml.allocated = True
        ml.addr = ps_base + bank * 2048
        return h

    ps_a = psum("ps_a", 0)
    ps_b = psum("ps_b", 2)
    ps_c = psum("ps_c", 4)
    ps_d = psum("ps_d", 6)
    scU = [psum("scU0", 0), psum("scU1", 2)]
    pv = [psum("pv0", 4, F32, 512), psum("pv1", 6, F32, 512)]
    sm = [psum("sm0", 5, F32, 512), psum("sm1", 7, F32, 512)]
    ps_tr = psum("ps_tr", 4, BF16, 1024)
    ps_tr2 = psum("ps_tr2", 6, BF16, 128)

    all_sems = ("T", "V", "S", "P") + DMA_SEMS
    sems = {n: ctx.enter_context(nc.semaphore(n)) for n in all_sems}
    S = Sched()
    S.sem("T", "tensor"); S.sem("V", "vector"); S.sem("S", "scalar")
    S.sem("P", "gpsimd")
    for n in DMA_SEMS:
        S.sem(n, "gpsimd" if n in ("C", "G") else ("scalar" if n.startswith("L") else "sync"))

    S.add("sync", lambda e: e.dma_start(out=ident[:, :], in_=identd), inc="Yc", inc_by=16)
    S.add("sync", lambda e: e.dma_start(out=ones[:, :], in_=onesd), inc="Yc", inc_by=16)
    consts_tok = S.add("sync", lambda e: e.dma_start(out=bb[:, :], in_=bbd), inc="Yc", inc_by=16)

    def ln_stats_mms(rhs_val_fn, rhs_sq_fn, val_tok, sq_tok, fc, extra=()):
        out_tok = None
        for (h0, h1) in HALVES:
            w = [val_tok, consts_tok] + (list(extra) if fc == 0 else [])
            S.add("tensor",
                  (lambda e, h0=h0, h1=h1: e.matmul(
                      ps_c[:, h0:h1], ones[:, :], rhs_val_fn(h0, h1),
                      start=(fc == 0), stop=(fc == NF - 1))),
                  waits=w)
        for i, (h0, h1) in enumerate(HALVES):
            out_tok = S.add("tensor",
                            (lambda e, h0=h0, h1=h1: e.matmul(
                                ps_d[:, h0:h1], ones[:, :], rhs_sq_fn(h0, h1),
                                start=(fc == 0), stop=(fc == NF - 1))),
                            waits=[sq_tok], inc="T" if i == 1 else None)
        return out_tok

    def ln_minv(stats_tok):
        S.add("vector",
              lambda e: e.tensor_scalar_mul(mstat[:, :], ps_c[:, :], INV_D),
              waits=[stats_tok])
        S.add("vector",
              lambda e: e.tensor_mul(invstat[:, :], mstat[:, :], mstat[:, :]))
        t2 = S.add("vector",
                   lambda e: e.scalar_tensor_tensor(
                       scr[:, :], ps_d[:, :], INV_D, invstat[:, :],
                       ALU.mult, ALU.subtract),
                   inc="V")
        t3 = S.add("scalar",
                   lambda e: e.activation(invstat[:, :], scr[:, :], AF.Sqrt,
                                          bias=EPS, scale=np.float32(1.0)),
                   waits=[t2], inc="S")
        return S.add("vector",
                     lambda e: e.reciprocal(scr[:, :], invstat[:, :]),
                     waits=[t3], inc="V")

    cast_tok = [None] * NF
    p1_sq_mm = [None] * NF
    grp_cast = [None] * 8
    grp_load = [None] * 8
    for g in range(8):
        w = [grp_cast[g - 2]] if g >= 2 else []
        grp_load[g] = S.add(
            "sync",
            (lambda e, g=g: e.dma_start(
                out=xstage[g % 2][:, :].rearrange("p (s t) -> p s t", s=4),
                in_=xT[g * 512:(g + 1) * 512, :].rearrange("(s p) t -> p s t", p=128))),
            waits=w, inc=f"Yx{g % 2}", inc_by=16)
        for s in range(4):
            fc = g * 4 + s
            cast_tok[fc] = S.add(
                "vector",
                (lambda e, g=g, s=s, fc=fc: e.tensor_copy(
                    A[:, fc * 1024:(fc + 1) * 1024],
                    xstage[g % 2][:, s * 1024:(s + 1) * 1024])),
                waits=[grp_load[g]], inc="V")
            wsq = [p1_sq_mm[fc - 2]] if fc >= 2 else []
            sq_tok = S.add(
                "scalar",
                (lambda e, fc=fc: e.activation(
                    sq1[fc % 2][:, :], A[:, fc * 1024:(fc + 1) * 1024], AF.Square)),
                waits=wsq + [cast_tok[fc]], inc="S")
            p1_sq_mm[fc] = ln_stats_mms(
                (lambda h0, h1, fc=fc: A[:, fc * 1024 + h0: fc * 1024 + h1]),
                (lambda h0, h1, fc=fc: sq1[fc % 2][:, h0:h1]),
                cast_tok[fc], sq_tok, fc)
        grp_cast[g] = cast_tok[g * 4 + 3]
    minv1 = ln_minv(p1_sq_mm[NF - 1])
    norm1 = [None] * NF
    for fc in range(NF):
        S.add("vector",
              (lambda e, fc=fc: e.tensor_sub(
                  invstat[:, :], A[:, fc * 1024:(fc + 1) * 1024], mstat[:, :])),
              waits=[minv1])
        norm1[fc] = S.add(
            "vector",
            (lambda e, fc=fc: e.tensor_mul(
                A[:, fc * 1024:(fc + 1) * 1024], invstat[:, :], scr[:, :])),
            inc="V")

    prev_user = [None, None]
    prev_evt = [None, None]

    def emit_wchunk(oc, wsrc, dest, per_ic_gate=None):
        par = oc % 2
        ld = S.add(
            "gpsimd",
            (lambda e: e.dma_start(out=wbf[par][:, :], in_=wsrc[oc])),
            waits=[prev_user[par]], inc="G", inc_by=16)
        prev_user[par] = None
        mm = None
        for ic in range(NF):
            for i, (h0, h1) in enumerate(HALVES):
                w = []
                if i == 0 and per_ic_gate is not None:
                    w.append(per_ic_gate[ic])
                if ic == 0 and i == 0:
                    w.append(ld)
                    if prev_evt[par] is not None:
                        w.append(prev_evt[par])
                last = (ic == NF - 1 and i == 1)
                mm = S.add(
                    "tensor",
                    (lambda e, ic=ic, h0=h0, h1=h1: e.matmul(
                        (ps_a if par == 0 else ps_b)[:, h0:h1],
                        wbf[par][:, ic * 128:(ic + 1) * 128],
                        A[:, ic * 1024 + h0: ic * 1024 + h1],
                        start=(ic == 0), stop=(ic == NF - 1))),
                    waits=w, inc="T" if last else None)
        prev_user[par] = mm
        ev = dest(oc, (ps_a if par == 0 else ps_b), mm)
        prev_evt[par] = ev
        return mm, ev

    k_store = [None] * NF

    def k_dest(oc, pst, mm_tok):
        w = [mm_tok]
        if oc >= 2:
            w.append(k_store[oc - 2])
        ev = S.add("vector",
                   (lambda e, oc=oc, pst=pst: e.tensor_copy(kev[oc % 2][:, :], pst[:, :])),
                   waits=w, inc="V")
        k_store[oc] = S.add(
            "scalar",
            (lambda e, oc=oc: e.dma_start(
                out=ag_in2[oc // 8, (oc % 8) * 128:(oc % 8 + 1) * 128, :],
                in_=kev[oc % 2][:, :])),
            waits=[ev], inc=f"Lk{oc % 2}", inc_by=16)
        return ev

    v_store = [None] * NF
    v_tr = [None] * NF
    v_ev2 = [None] * NF

    def v_dest(oc, pst, mm_tok):
        w = [mm_tok]
        if oc >= 1:
            w.append(v_tr[oc - 1])
        if oc == 0:
            w.append(minv1)
        ev1 = S.add("vector",
                    (lambda e, pst=pst: e.tensor_copy(vev[:, :], pst[:, :])),
                    waits=w, inc="V")
        for j in range(8):
            w2 = [ev1]
            if j == 0 and oc >= 1:
                w2.append(v_ev2[oc - 1])
            v_tr[oc] = S.add(
                "tensor",
                (lambda e, j=j: e.transpose(
                    ps_tr[:, j * 128:(j + 1) * 128], vev[:, j * 128:(j + 1) * 128],
                    ident[:, :])),
                waits=w2, inc="T" if j == 7 else None)
        w3 = [v_tr[oc]]
        if oc >= 1:
            w3.append(v_store[oc - 1])
        v_ev2[oc] = S.add("vector",
                          (lambda e: e.tensor_copy(vtr[:, :], ps_tr[:, :])),
                          waits=w3, inc="V")
        v_store[oc] = S.add(
            "scalar",
            (lambda e, oc=oc: e.dma_start(
                out=ag_in2[oc // 8, 1024:2048,
                           (oc % 8) * 128:(oc % 8 + 1) * 128].rearrange(
                               "(j p) f -> p j f", p=128),
                in_=vtr[:, :].rearrange("p (j f) -> p j f", f=128))),
            waits=[v_ev2[oc]], inc="Lv", inc_by=16)
        return ev1

    def q_dest(oc, pst, mm_tok):
        return S.add("vector",
                     (lambda e, oc=oc, pst=pst: e.tensor_copy(
                         B[:, oc * 1024:(oc + 1) * 1024], pst[:, :])),
                     waits=[mm_tok], inc="V")

    ag_tok = [None] * 4
    q_evt = [None] * NF
    for sl in range(4):
        for oc in range(8 * sl, 8 * sl + 8):
            emit_wchunk(oc, WKs, k_dest, per_ic_gate=norm1 if oc == 0 else None)
        for oc in range(8 * sl, 8 * sl + 8):
            emit_wchunk(oc, WVs, v_dest)
        ag_tok[sl] = S.add(
            "gpsimd",
            (lambda e, sl=sl: e.collective_compute(
                "AllGather", ALU.bypass,
                replica_groups=[[0, 1, 2, 3], [4, 5, 6, 7]],
                ins=[ag_in2[sl].opt()], outs=[ag_out2[sl].opt()])),
            waits=[k_store[8 * sl + 6], k_store[8 * sl + 7], v_store[8 * sl + 7]],
            inc="C")
    for oc in range(NF):
        _, q_evt[oc] = emit_wchunk(oc, WQs, q_dest)
    q_done = q_evt[NF - 1]

    kt_ld = [None] * NH
    kv_ld = [None] * NH
    exp_tok = {}
    sct = {}
    pvt = {}
    aev_tok = [None] * 64
    kt_lastuse = [None] * NH
    vt_lastuse = [None] * NH
    ag_all = S.final("C")

    def emit_pvsum(u_):
        g_ = u_ // NU
        h_ = g_ // 2
        ph_ = g_ % 2
        j_ = u_ % NU
        first = (j_ == 0)
        last = (j_ == NU - 1)
        tk = None
        for cj in range(2):
            c = 2 * j_ + cj
            w = []
            if cj == 0:
                w.append(exp_tok[u_])
                if first:
                    if g_ >= 2:
                        w.append(aev_tok[g_ - 2])
                    else:
                        w.append(minv1)
                    if g_ % 2 == 0:
                        w.append(kv_ld[h_])
            S.add("tensor",
                  (lambda e, h_=h_, ph_=ph_, c=c, u_=u_, cj=cj, first=first, last=last:
                   e.matmul(pv[ph_][:, :], vt[h_ % 2][:, c * 128:(c + 1) * 128],
                            et[u_ % 2][:, cj * 512:(cj + 1) * 512],
                            start=(first and cj == 0), stop=(last and cj == 1))),
                  waits=w)
            tk = S.add("tensor",
                       (lambda e, ph_=ph_, u_=u_, cj=cj, first=first, last=last:
                        e.matmul(sm[ph_][:, :], ones[:, :],
                                 et[u_ % 2][:, cj * 512:(cj + 1) * 512],
                                 start=(first and cj == 0), stop=(last and cj == 1))),
                       inc="T" if cj == 1 else None)
        pvt[u_] = tk
        if last:
            if g_ % 2 == 1:
                vt_lastuse[h_] = tk
            q0 = (g_ % 2) * 512
            S.add("vector",
                  (lambda e, ph_=ph_: e.reciprocal(invat[:, :], sm[ph_][:, :])),
                  waits=[tk])
            aev_tok[g_] = S.add(
                "vector",
                (lambda e, h_=h_, ph_=ph_, q0=q0: e.tensor_mul(
                    A[:, h_ * 1024 + q0: h_ * 1024 + q0 + 512],
                    pv[ph_][:, :], invat[:, :])),
                inc="V")

    for h in range(NH):
        sl = h // 8
        w = [ag_tok[sl]]
        if h >= 2:
            w.append(kt_lastuse[h - 2])
        for r in range(4):
            kt_ld[h] = S.add(
                "sync",
                (lambda e, h=h, r=r: e.dma_start(
                    out=kt[h % 2][:, r * 1024:(r + 1) * 1024],
                    in_=ag_out2[h // 8, r, (h % 8) * 128:(h % 8 + 1) * 128, :])),
                waits=w if r == 0 else [], inc=f"Ykt{h % 2}", inc_by=16)
        w2 = [ag_tok[sl]]
        if h >= 2:
            w2.append(vt_lastuse[h - 2])
        for r in range(4):
            kv_ld[h] = S.add(
                "sync",
                (lambda e, h=h, r=r: e.dma_start(
                    out=vt[h % 2][:, r * 1024:(r + 1) * 1024].rearrange(
                        "p (kc f) -> p kc f", f=128),
                    in_=ag_out2[h // 8, r, 1024:2048,
                                (h % 8) * 128:(h % 8 + 1) * 128].rearrange(
                                    "(kc p) f -> p kc f", p=128))),
                waits=w2 if r == 0 else [], inc=f"Yvt{h % 2}", inc_by=16)
        for qh in range(2):
            g = h * 2 + qh
            q0 = qh * 512
            for j in range(NU):
                u = g * NU + j
                for cj in range(2):
                    c = 2 * j + cj
                    w3 = []
                    if cj == 0:
                        if u >= 2:
                            w3.append(exp_tok[u - 2])
                        if u == 0:
                            w3.append(q_done)
                        if j == 0 and qh == 0:
                            w3.append(kt_ld[h])
                    tk = S.add(
                        "tensor",
                        (lambda e, h=h, c=c, cj=cj, u=u, q0=q0: e.matmul(
                            scU[u % 2][:, cj * 512:(cj + 1) * 512],
                            kt[h % 2][:, c * 128:(c + 1) * 128],
                            B[:, h * 1024 + q0: h * 1024 + q0 + 512],
                            start=True, stop=True)),
                        waits=w3, inc="T" if cj == 1 else None)
                sct[u] = tk
                if j == NU - 1 and qh == 1:
                    kt_lastuse[h] = tk
                w4 = [sct[u]]
                if u >= 2:
                    w4.append(pvt[u - 2])
                if u < 2:
                    w4.append(ag_all)
                exp_tok[u] = S.add(
                    "scalar",
                    (lambda e, u=u: e.activation(
                        et[u % 2][:, :], scU[u % 2][:, :],
                        AF.Exp, bias=EXP_BIAS, scale=INV_SQRT_DK)),
                    waits=w4, inc="S")
                if u >= 1:
                    emit_pvsum(u - 1)
    emit_pvsum(64 * NU - 1)
    attn_done = aev_tok[63]
    last_exp = exp_tok[64 * NU - 1]

    o_ld = [None] * NF
    o_mm = [None] * NF
    o_vo = [None] * NF
    o_st = [None] * NF
    for oc in range(NF):
        o_ld[oc] = S.add(
            "gpsimd",
            (lambda e, oc=oc: e.dma_start(out=wbf[oc % 2][:, :], in_=WOs[oc])),
            waits=[prev_user[oc % 2]], inc="G", inc_by=16)
        prev_user[oc % 2] = None
        for ic in range(NF):
            for i, (h0, h1) in enumerate(HALVES):
                w = []
                if ic == 0 and i == 0:
                    w = [o_ld[oc], attn_done, last_exp]
                    if oc >= 2:
                        w.append(o_vo[oc - 2])
                last = (ic == NF - 1 and i == 1)
                o_mm[oc] = S.add(
                    "tensor",
                    (lambda e, oc=oc, ic=ic, h0=h0, h1=h1: e.matmul(
                        (ps_a if oc % 2 == 0 else ps_b)[:, h0:h1],
                        wbf[oc % 2][:, ic * 128:(ic + 1) * 128],
                        A[:, ic * 1024 + h0: ic * 1024 + h1],
                        start=(ic == 0), stop=(ic == NF - 1))),
                    waits=w, inc="T" if last else None)
        prev_user[oc % 2] = o_mm[oc]
        o_vo[oc] = S.add(
            "scalar",
            (lambda e, oc=oc: e.copy(
                B[:, oc * 1024:(oc + 1) * 1024],
                (ps_a if oc % 2 == 0 else ps_b)[:, :])),
            waits=[o_mm[oc]], inc="S")
        w = [o_vo[oc]]
        if oc >= 2:
            w.append(o_st[oc - 2])
        sq_tok = S.add(
            "scalar",
            (lambda e, oc=oc: e.activation(
                sqo[oc % 2][:, :], B[:, oc * 1024:(oc + 1) * 1024], AF.Square)),
            waits=w, inc="S")
        extra = [attn_done] if oc == 0 else []
        o_st[oc] = ln_stats_mms(
            (lambda h0, h1, oc=oc: B[:, oc * 1024 + h0: oc * 1024 + h1]),
            (lambda h0, h1, oc=oc: sqo[oc % 2][:, h0:h1]),
            o_vo[oc], sq_tok, oc, extra=extra)
    minv6 = ln_minv(o_st[NF - 1])

    qka_ld = [None] * 4
    for g4 in range(3):
        qka_ld[g4] = S.add(
            "gpsimd",
            (lambda e, g4=g4: e.dma_start(
                out=qkabf[:, g4 * 5120:(g4 + 1) * 5120],
                in_=QKAsd[:, g4 * 5120:(g4 + 1) * 5120])),
            waits=[attn_done] if g4 == 0 else [], inc="G", inc_by=16)
    bmp_ld = S.add("gpsimd",
                   (lambda e: e.dma_start(out=BmP[:, :], in_=BmPd)),
                   waits=[], inc="G", inc_by=16)

    xr_ld = [None] * NF
    x2_tok = [None] * NF
    for oc in range(NF):
        if oc < 4:
            w = [o_mm[30 + (oc // 2)]]
        else:
            w = [x2_tok[oc - 4]]
        xr_ld[oc] = S.add(
            "sync",
            (lambda e, oc=oc: e.dma_start(
                out=xr[oc % 4][:, :], in_=xT[oc * 128:(oc + 1) * 128, :])),
            waits=w, inc=f"Yxr{oc % 4}", inc_by=16)
        if oc % 2 == 0:
            S.add("vector",
                  (lambda e, oc=oc: e.tensor_sub(
                      invstat[:, :], B[:, oc * 1024:(oc + 1) * 1024], mstat[:, :])),
                  waits=[minv6, xr_ld[oc]])
            S.add("vector",
                  (lambda e: e.tensor_mul(sqo[0][:, :], invstat[:, :], scr[:, :])))
            x2_tok[oc] = S.add(
                "vector",
                (lambda e, oc=oc: e.tensor_add(
                    B[:, oc * 1024:(oc + 1) * 1024], sqo[0][:, :], xr[oc % 4][:, :])),
                inc="V")
        else:
            S.add("gpsimd",
                  (lambda e, oc=oc: e.tensor_sub(
                      pt0[:, :], B[:, oc * 1024:(oc + 1) * 1024], mstat[:, :])),
                  waits=[minv6, xr_ld[oc]])
            S.add("gpsimd",
                  (lambda e: e.tensor_mul(pt0[:, :], pt0[:, :], scr[:, :])))
            x2_tok[oc] = S.add(
                "gpsimd",
                (lambda e, oc=oc: e.tensor_add(
                    B[:, oc * 1024:(oc + 1) * 1024], pt0[:, :], xr[oc % 4][:, :])),
                inc="P")
    x2_done = x2_tok[NF - 1]

    p7_mm = [None] * NF
    for oc in range(NF):
        w = [x2_tok[oc]]
        if oc >= 2:
            w.append(p7_mm[oc - 2])
        sq_tok = S.add(
            "scalar",
            (lambda e, oc=oc: e.activation(
                sqo[oc % 2][:, :], B[:, oc * 1024:(oc + 1) * 1024], AF.Square)),
            waits=w, inc="S")
        p7_mm[oc] = ln_stats_mms(
            (lambda h0, h1, oc=oc: B[:, oc * 1024 + h0: oc * 1024 + h1]),
            (lambda h0, h1, oc=oc: sqo[oc % 2][:, h0:h1]),
            x2_tok[oc], sq_tok, oc, extra=[minv6])
    minv7 = ln_minv(p7_mm[NF - 1])
    ln2_v = None
    ln2_p = None
    for oc in range(NF):
        if oc % 2 == 0:
            S.add("vector",
                  (lambda e, oc=oc: e.tensor_sub(
                      invstat[:, :], B[:, oc * 1024:(oc + 1) * 1024], mstat[:, :])),
                  waits=[minv7, o_mm[NF - 1]])
            ln2_v = S.add(
                "vector",
                (lambda e, oc=oc: e.tensor_mul(
                    A[:, oc * 1024:(oc + 1) * 1024], invstat[:, :], scr[:, :])),
                inc="V")
        else:
            S.add("gpsimd",
                  (lambda e, oc=oc: e.tensor_sub(
                      pt0[:, :], B[:, oc * 1024:(oc + 1) * 1024], mstat[:, :])),
                  waits=[minv7])
            ln2_p = S.add(
                "gpsimd",
                (lambda e, oc=oc: e.tensor_mul(
                    A[:, oc * 1024:(oc + 1) * 1024], pt0[:, :], scr[:, :])),
                inc="P")
    ln2_fin = [ln2_v, ln2_p]

    qka_ld[3] = S.add(
        "gpsimd",
        (lambda e: e.dma_start(
            out=qkabf[:, 3 * 5120:4 * 5120],
            in_=QKAsd[:, 3 * 5120:4 * 5120])),
        waits=list(ln2_fin), inc="G", inc_by=16)

    qkh_mm = [None] * 8
    qkh_ev = [None] * 8
    scrmul = [None] * 8
    gt_ev = [None] * 8
    for tc in range(8):
        pq = ps_a if tc % 2 == 0 else ps_b
        hofs = 0 if tc % 2 == 0 else 512
        for ic in range(NF):
            lic, g4 = ic % 8, ic // 8
            w = [qka_ld[g4]] + (list(ln2_fin) if ic == 0 else [])
            if ic == 0 and tc >= 2:
                w += [scrmul[tc - 2], qkh_ev[tc - 2]]
            if ic == 0 and tc < 2:
                w.append(o_mm[NF - 1])
            S.add("tensor",
                  (lambda e, tc=tc, ic=ic, g4=g4, lic=lic, pq=pq: e.matmul(
                      pq[:, 0:256],
                      A[:, ic * 1024 + tc * 128: ic * 1024 + (tc + 1) * 128],
                      qkabf[:, g4 * 5120 + lic * 640: g4 * 5120 + lic * 640 + 256],
                      start=(ic == 0), stop=(ic == NF - 1))),
                  waits=w)
            S.add("tensor",
                  (lambda e, tc=tc, ic=ic, g4=g4, lic=lic, pq=pq: e.matmul(
                      pq[:, 512:768],
                      A[:, ic * 1024 + tc * 128: ic * 1024 + (tc + 1) * 128],
                      qkabf[:, g4 * 5120 + lic * 640 + 256: g4 * 5120 + lic * 640 + 512],
                      start=(ic == 0), stop=(ic == NF - 1))))
            qkh_mm[tc] = S.add(
                "tensor",
                (lambda e, tc=tc, ic=ic, g4=g4, lic=lic, hofs=hofs: e.matmul(
                    ps_c[:, hofs: hofs + 128],
                    A[:, ic * 1024 + tc * 128: ic * 1024 + (tc + 1) * 128],
                    qkabf[:, g4 * 5120 + lic * 640 + 512: g4 * 5120 + lic * 640 + 640],
                    start=(ic == 0), stop=(ic == NF - 1))),
                inc="T" if ic == NF - 1 else None)
        kcp = S.add("scalar",
                    (lambda e, tc=tc, pq=pq: e.copy(
                        kbuf[:, 0:256] if tc % 2 == 0 else kbuf[:, 1024:1280],
                        pq[:, 512:768])),
                    waits=[qkh_mm[tc]], inc="S")
        qkh_ev[tc] = S.add(
            "scalar",
            (lambda e, tc=tc, hofs=hofs: e.copy(
                hbuf[:, tc % 2 * 512: tc % 2 * 512 + 128], ps_c[:, hofs: hofs + 128])),
            inc="S")
        scrmul[tc] = S.add(
            "vector",
            (lambda e, tc=tc, pq=pq: e.tensor_mul(
                scr[:, 0:256], pq[:, 0:256],
                kbuf[:, 0:256] if tc % 2 == 0 else kbuf[:, 1024:1280])),
            waits=[kcp], inc="V")
        S.add("vector",
              (lambda e: e.reduce_sum(
                  sc8[:, 0:8],
                  scr[:, 0:256].rearrange("p (ex k) -> p ex k", ex=8), axis=AX.X)))
        S.add("vector", lambda e: e.tensor_scalar_mul(w8[:, :], sc8[:, :], INV_SQRT_DK_FFN))
        S.add("vector", lambda e: e.reduce_max(smax[:, 0:1], w8[:, :], axis=AX.X))
        tse = S.add("vector",
                    lambda e: e.tensor_scalar_sub(sc8[:, :], w8[:, :], smax[:, 0:1]),
                    inc="V")
        texp = S.add("scalar",
                     (lambda e: e.activation(w8[:, :], sc8[:, :], AF.Exp)),
                     waits=[tse], inc="S")
        S.add("vector", lambda e: e.reduce_sum(ssum[:, 0:1], w8[:, :], axis=AX.X),
              waits=[texp])
        S.add("vector", lambda e: e.reciprocal(srec[:, 0:1], ssum[:, 0:1]))
        S.add("vector", lambda e: e.tensor_scalar_mul(sc8[:, :], w8[:, :], srec[:, 0:1]))
        for ex in range(8):
            S.add("vector",
                  (lambda e, ex=ex: e.tensor_scalar_mul(
                      scr[:, 512 + ex * 16: 512 + (ex + 1) * 16],
                      ones[:, 0:16], sc8[:, ex:ex + 1])))
        gv = S.add("vector",
                   (lambda e, tc=tc: e.tensor_mul(
                       gtile[:, :], hbuf[:, tc % 2 * 512: tc % 2 * 512 + 128],
                       scr[:, 512:640])),
                   waits=[qkh_ev[tc]], inc="V")
        w = [gv]
        if tc >= 1:
            w.append(gt_ev[tc - 1])
        if tc == 0:
            w.append(attn_done)
        gtr = S.add("tensor",
                    (lambda e: e.transpose(ps_tr2[:, :], gtile[:, :], ident[:, :])),
                    waits=w, inc="T")
        gt_ev[tc] = S.add("scalar",
                          (lambda e, tc=tc: e.copy(
                              gt[:, tc * 128:(tc + 1) * 128], ps_tr2[:, :])),
                          waits=[gtr], inc="S")
    gt_done = gt_ev[7]

    st8 = {}
    d_vo_prev = {}
    vout_prev = {}
    for hf in range(2):
        t0 = hf * 512
        d_ld = [None] * NF
        d_mm = [None] * NF
        d_vo = [None] * NF
        d_st = [None] * NF
        vout_cur = {}
        for oc in range(NF):
            w = [prev_user[oc % 2]]
            if hf == 0 and oc < 2:
                w.append(x2_tok[29 if oc == 0 else 31])
            d_ld[oc] = S.add(
                "gpsimd",
                (lambda e, oc=oc: e.dma_start(out=wbf[oc % 2][:, :], in_=Ws[oc])),
                waits=w, inc="G", inc_by=16)
            prev_user[oc % 2] = None
            po = 0 if oc % 2 == 0 else 512
            w = [bmp_ld, gt_done]
            if oc >= 2:
                w.append(d_vo[oc - 2])
            elif hf == 1:
                w.append(d_vo_prev[NF - 2 + oc])
            S.add("tensor",
                  (lambda e, oc=oc, po=po, t0=t0: e.matmul(
                      ps_a[:, po: po + 512],
                      BmP[:, oc * 128:(oc + 1) * 128],
                      gt[:, t0:t0 + 512],
                      start=True, stop=False)),
                  waits=w)
            for ic in range(NF):
                w = [d_ld[oc]] if ic == 0 else []
                last = (ic == NF - 1)
                d_mm[oc] = S.add(
                    "tensor",
                    (lambda e, oc=oc, ic=ic, po=po, t0=t0: e.matmul(
                        ps_a[:, po: po + 512],
                        wbf[oc % 2][:, ic * 128:(ic + 1) * 128],
                        A[:, ic * 1024 + t0: ic * 1024 + t0 + 512],
                        start=False, stop=last)),
                    waits=w, inc="T" if last else None)
            prev_user[oc % 2] = d_mm[oc]
            w = [d_mm[oc]]
            if hf == 1:
                w.append(vout_prev[oc])
            d_vo[oc] = S.add(
                "scalar",
                (lambda e, oc=oc, po=po: e.add(
                    vhalf[:, oc * 512:(oc + 1) * 512],
                    ps_a[:, po: po + 512], bb[:, oc:oc + 1])),
                waits=w, inc="S")
            w = [d_vo[oc]]
            if oc >= 1:
                w.append(d_st[oc - 1])
            sqv = S.add(
                "scalar",
                (lambda e, oc=oc: e.activation(
                    sq8[:, :], vhalf[:, oc * 512:(oc + 1) * 512], AF.Square)),
                waits=w, inc="S")
            S.add("tensor",
                  (lambda e, oc=oc: e.matmul(
                      ps_b[:, 0:512], ones[:, :],
                      vhalf[:, oc * 512:(oc + 1) * 512],
                      start=(oc == 0), stop=(oc == NF - 1))),
                  waits=[d_vo[oc]])
            d_st[oc] = S.add(
                "tensor",
                (lambda e, oc=oc: e.matmul(
                    ps_b[:, 512:1024], ones[:, :], sq8[:, :],
                    start=(oc == 0), stop=(oc == NF - 1))),
                waits=[sqv], inc="T")
        d_vo_prev = {i: d_vo[i] for i in range(NF)}
        w = [d_st[NF - 1]]
        if hf == 1:
            w += [st8[(0, NF - 2)], st8[(0, NF - 1)]]
        S.add("vector",
              (lambda e: e.tensor_scalar_mul(mstat[:, 0:512], ps_b[:, 0:512], INV_D)),
              waits=w)
        S.add("vector",
              (lambda e: e.tensor_mul(invstat[:, 0:512], mstat[:, 0:512], mstat[:, 0:512])))
        t2 = S.add("vector",
                   (lambda e: e.scalar_tensor_tensor(
                       scr[:, 0:512], ps_b[:, 512:1024], INV_D, invstat[:, 0:512],
                       ALU.mult, ALU.subtract)),
                   inc="V")
        t3 = S.add("scalar",
                   (lambda e: e.activation(invstat[:, 0:512], scr[:, 0:512], AF.Sqrt,
                                           bias=EPS, scale=np.float32(1.0))),
                   waits=[t2], inc="S")
        minv8 = S.add("vector",
                      (lambda e: e.reciprocal(mstat[:, 512:1024], invstat[:, 0:512])),
                      waits=[t3], inc="V")
        for oc in range(NF):
            w = [minv8]
            if oc >= 2:
                w.append(st8[(hf, oc - 2)])
            oofs = 0 if oc % 2 == 0 else 512
            if hf == 1 and oc % 2 == 1:
                S.add("gpsimd",
                      (lambda e, oc=oc: e.tensor_sub(
                          pt0[:, 0:512], vhalf[:, oc * 512:(oc + 1) * 512],
                          mstat[:, 0:512])),
                      waits=w)
                S.add("gpsimd",
                      (lambda e: e.tensor_mul(pt0[:, 512:1024], pt0[:, 0:512],
                                              mstat[:, 512:1024])))
                vout = S.add(
                    "gpsimd",
                    (lambda e, oc=oc, oofs=oofs, t0=t0: e.tensor_add(
                        invstat[:, oofs:oofs + 512], pt0[:, 512:1024],
                        B[:, oc * 1024 + t0: oc * 1024 + t0 + 512])),
                    inc="P")
            else:
                S.add("vector",
                      (lambda e, oc=oc: e.tensor_sub(
                          scr[:, 0:512], vhalf[:, oc * 512:(oc + 1) * 512],
                          mstat[:, 0:512])),
                      waits=w)
                S.add("vector",
                      (lambda e: e.tensor_mul(scr[:, 512:1024], scr[:, 0:512],
                                              mstat[:, 512:1024])))
                vout = S.add(
                    "vector",
                    (lambda e, oc=oc, oofs=oofs, t0=t0: e.tensor_add(
                        invstat[:, oofs:oofs + 512], scr[:, 512:1024],
                        B[:, oc * 1024 + t0: oc * 1024 + t0 + 512])),
                    inc="V")
            vout_cur[oc] = vout
            st8[(hf, oc)] = S.add(
                "scalar",
                (lambda e, oc=oc, oofs=oofs, t0=t0: e.dma_start(
                    out=outT[oc * 128:(oc + 1) * 128, t0:t0 + 512],
                    in_=invstat[:, oofs:oofs + 512])),
                waits=[vout], inc=f"Lo{oc % 2}", inc_by=16)
        vout_prev = vout_cur
    S.add("scalar", None, waits=[S.final("Lo0"), S.final("Lo1")])

    for ename in S.ENGINES:
        pruned = []
        for emit, waits, inc, inc_by in S.ops[ename]:
            w2 = [(snm, v) for (snm, v) in waits
                  if not (S.producer[snm] == ename and snm not in DMA_SEMS)]
            pruned.append((emit, w2, inc, inc_by))
        S.ops[ename] = pruned

    def emitter(ops, self_chain_sem=None):
        def run(eng):
            last = {}
            n_self = 0
            for emit, waits, inc, inc_by in ops:
                for (snm, val) in waits:
                    if last.get(snm, -1) >= val:
                        continue
                    eng.wait_ge(sems[snm], val)
                    last[snm] = val
                if emit is None:
                    continue
                if self_chain_sem is not None and n_self >= 1:
                    eng.wait_ge(sems[self_chain_sem], n_self)
                inst = emit(eng)
                if inc is not None:
                    inst.then_inc(sems[inc], inc_by)
                    if inc == self_chain_sem:
                        n_self += inc_by
        return run

    with nc.Block() as block:
        block.sync(emitter(S.ops["sync"]))
        block.gpsimd(emitter(S.ops["gpsimd"]))
        block.scalar(emitter(S.ops["scalar"]))
        block.vector(emitter(S.ops["vector"], self_chain_sem="V"))
        block.tensor(emitter(S.ops["tensor"]))

    nc._keepalive = ctx
    return nc


def _chunk(wT):
    return np.ascontiguousarray(
        wT.reshape(32, 128, 32, 128).transpose(2, 1, 0, 3).reshape(32, 128, 4096))


def _host_stage(inputs):
    bf = ml_dtypes.bfloat16
    x = np.asarray(inputs["L0_attn_x"], dtype=np.float32)
    wqka = np.concatenate([
        np.asarray(inputs["L0_ffn_Wq"], np.float32),
        np.asarray(inputs["L0_ffn_Wk"], np.float32),
        np.asarray(inputs["L0_ffn_A"], np.float32).reshape(128, 4096)], axis=0).T
    stage = {
        "WQs": _chunk(np.asarray(inputs["L0_attn_w_q"], np.float32).T),
        "WKs": _chunk(np.asarray(inputs["L0_attn_w_k"], np.float32).T),
        "WVs": _chunk(np.asarray(inputs["L0_attn_w_v"], np.float32).T),
        "WOs": _chunk(np.asarray(inputs["L0_attn_w_o"], np.float32).T),
        "Ws": _chunk(np.asarray(inputs["L0_ffn_W"], np.float32).T),
        "QKAsd": np.ascontiguousarray(
            wqka.reshape(4, 8, 128, 640).transpose(2, 0, 1, 3).reshape(128, 20480)),
        "BmPd": np.ascontiguousarray(
            np.asarray(inputs["L0_ffn_Bm"], np.float32).transpose(0, 2, 1).reshape(128, 4096)),
        "bbd": np.ascontiguousarray(np.asarray(inputs["L0_ffn_b"], np.float32).reshape(32, 128).T),
        "identd": np.eye(128, dtype=bf),
        "onesd": np.ones((128, 128), dtype=bf),
    }
    in_maps = []
    for c in range(8):
        b, q = c // 4, c % 4
        m = dict(stage)
        m["xT"] = np.ascontiguousarray(x[b, q * 1024:(q + 1) * 1024, :].T)
        in_maps.append(m)
    return in_maps


def kernel(**inputs):
    in_maps = _host_stage(inputs)
    nc = build_nc()
    res = run_bass_kernel_spmd(nc, in_maps, core_ids=list(range(8)))
    out = np.empty((2, 4096, 4096), dtype=np.float32)
    for c in range(8):
        b, q = c // 4, c % 4
        out[b, q * 1024:(q + 1) * 1024, :] = res.results[c]["outT"].T
    return out


# version 06 (final)
# speedup vs baseline: 1.0230x; 1.0230x over previous
import numpy as np
import ml_dtypes
from contextlib import ExitStack

import concourse.bass as bass
import concourse.mybir as mybir
from concourse.bass_utils import run_bass_kernel_spmd

F32 = mybir.dt.float32
BF16 = mybir.dt.bfloat16
U8 = mybir.dt.uint8
AF = mybir.ActivationFunctionType
ALU = mybir.AluOpType
AX = mybir.AxisListType

D = 4096
T = 1024
NF = 32
NH = 32
NU = 16
INV_SQRT_DK = np.float32(1.0 / np.sqrt(128.0))
EXP_BIAS = np.float32(-3.0)
INV_SQRT_DK_FFN = float(1.0 / np.sqrt(32.0))
EPS = np.float32(1e-5)
INV_D = float(1.0 / 4096.0)

DMA_SEMS = ("Yc", "Yx0", "Yx1", "G", "Ykt0", "Ykt1", "Yvt0", "Yvt1",
            "Yxr0", "Yxr1", "Yxr2", "Yxr3",
            "Lk0", "Lk1", "Lv", "Lo0", "Lo1", "C")
HALVES = ((0, 512), (512, 1024))


class Sched:
    ENGINES = ("sync", "gpsimd", "scalar", "vector", "tensor")

    def __init__(self):
        self.ops = {e: [] for e in self.ENGINES}
        self.count = {}
        self.producer = {}

    def sem(self, name, producer):
        self.count[name] = 0
        self.producer[name] = producer

    def add(self, engine, emit, waits=(), inc=None, inc_by=1):
        if engine == "vector" and inc is None and emit is not None:
            inc = "V"
        tok = None
        if inc is not None:
            self.count[inc] += inc_by
            tok = (inc, self.count[inc])
        self.ops[engine].append((emit, [w for w in waits if w is not None], inc, inc_by))
        return tok

    def final(self, sem):
        return (sem, self.count[sem])


def build_nc():
    nc = bass.Bass(trn_type="TRN2", target_bir_lowering=False)

    xT = nc.dram_tensor("xT", [D, T], F32, kind="ExternalInput").ap()
    WQs = nc.dram_tensor("WQs", [NF, 128, D], F32, kind="ExternalInput").ap()
    WKs = nc.dram_tensor("WKs", [NF, 128, D], F32, kind="ExternalInput").ap()
    WVs = nc.dram_tensor("WVs", [NF, 128, D], F32, kind="ExternalInput").ap()
    WOs = nc.dram_tensor("WOs", [NF, 128, D], F32, kind="ExternalInput").ap()
    Ws = nc.dram_tensor("Ws", [NF, 128, D], F32, kind="ExternalInput").ap()
    QKAsd = nc.dram_tensor("QKAsd", [128, 20480], F32, kind="ExternalInput").ap()
    BmPd = nc.dram_tensor("BmPd", [128, D], F32, kind="ExternalInput").ap()
    bbd = nc.dram_tensor("bbd", [128, 32], F32, kind="ExternalInput").ap()
    identd = nc.dram_tensor("identd", [128, 128], BF16, kind="ExternalInput").ap()
    onesd = nc.dram_tensor("onesd", [128, 128], BF16, kind="ExternalInput").ap()
    ag_in2 = nc.dram_tensor("ag_in2", [4, 2048, T], BF16).ap()
    ag_out2 = nc.dram_tensor("ag_out2", [4, 4, 2048, T], BF16).ap()
    outT = nc.dram_tensor("outT", [D, T], F32, kind="ExternalOutput").ap()

    ctx = ExitStack()
    SBUF_TOTAL = 212832
    arena = ctx.enter_context(nc.sbuf_tensor("arena", [128, SBUF_TOTAL], U8))
    sb_base = nc.lookup_mloc(arena).addr

    def sb(name, shape, dtype, off):
        per_part = int(np.prod(shape[1:])) * mybir.dt.size(dtype)
        assert off % 32 == 0 and off + per_part <= SBUF_TOTAL, (name, off, per_part)
        h = nc.virtual_sbuf_tensor(name, list(shape), dtype)
        nc._tensor(h.name, h.shape, h.dtype, type="SB")
        ml = nc.lookup_mloc(h)
        ml.allocated = True
        ml.addr = sb_base + off
        return h

    C = 131072
    A = sb("A", [128, 32 * 1024], BF16, 0)
    B = sb("B", [128, 32 * 1024], BF16, 65536)
    wbf = [sb(f"wbf{i}", [128, 4096], BF16, C + i * 8192) for i in range(2)]
    xr = [sb(f"xr{i}", [128, 1024], F32, C + i * 4096) for i in range(4)]
    kt = [sb(f"kt{i}", [128, 4096], BF16, C + 16384 + i * 8192) for i in range(2)]
    vt = [sb(f"vt{i}", [128, 4096], BF16, C + 32768 + i * 8192) for i in range(2)]
    xstage = [sb(f"xstage{i}", [128, 4096], F32, C + 16384 + i * 16384) for i in range(2)]
    et = [sb(f"et{i}", [128, 1024], BF16, C + 49152 + i * 2048) for i in range(2)]
    invat = sb("invat", [128, 512], F32, C + 53248)
    sq1 = [sb(f"sq1_{i}", [128, 1024], BF16, C + 49152 + i * 2048) for i in range(2)]
    kev = [sb(f"kev{i}", [128, 1024], BF16, C + 55296 + i * 2048) for i in range(2)]
    vev = sb("vev", [128, 1024], BF16, C + 59392)
    vtr = sb("vtr", [128, 1024], BF16, C + 61440)
    qkabf = sb("qkabf", [128, 20480], BF16, C + 16384)
    pt0 = sb("pt0", [128, 1024], F32, C + 49152)
    vhalf = sb("vhalf", [128, 16384], BF16, C + 16384)
    BmP = sb("BmP", [128, 4096], BF16, C + 57344)
    sqo = [sb(f"sqo{i}", [128, 1024], BF16, C + 53248 + i * 2048) for i in range(2)]
    gt = sb("gt", [128, 1024], BF16, 196608)
    mstat = sb("mstat", [128, 1024], F32, 198656)
    invstat = sb("invstat", [128, 1024], F32, 202752)
    scr = sb("scr", [128, 1024], F32, 206848)
    kbuf = sb("kbuf", [128, 2048], BF16, 198656)
    gtile = sb("gtile", [128, 128], BF16, 202752 + 1024)
    hbuf = sb("hbuf", [128, 1024], BF16, 202752 + 2048)
    ident = sb("ident", [128, 128], BF16, 210944)
    ones = sb("ones", [128, 128], BF16, 211200)
    bb = sb("bb", [128, 32], F32, 211456)
    sc8 = sb("sc8", [128, 8], F32, 211584)
    smax = sb("smax", [128, 8], F32, 211616)
    ssum = sb("ssum", [128, 8], F32, 211648)
    w8 = sb("w8", [128, 8], F32, 211680)
    srec = sb("srec", [128, 8], F32, 211712)
    sq8 = sb("sq8", [128, 512], BF16, 211744)

    psarena = ctx.enter_context(nc.psum_tensor("psarena", [128, 4096], F32))
    ps_base = nc.lookup_mloc(psarena).addr

    def psum(name, bank, dtype=F32, width=1024):
        h = nc.virtual_psum_tensor(name, [128, width], dtype)
        nc._tensor(h.name, h.shape, h.dtype, type="PSUM")
        ml = nc.lookup_mloc(h)
        ml.allocated = True
        ml.addr = ps_base + bank * 2048
        return h

    ps_a = psum("ps_a", 0)
    ps_b = psum("ps_b", 2)
    ps_c = psum("ps_c", 4)
    ps_d = psum("ps_d", 6)
    scU = [psum("scU0", 0), psum("scU1", 2)]
    pv = [psum("pv0", 4, F32, 512), psum("pv1", 6, F32, 512)]
    sm = [psum("sm0", 5, F32, 512), psum("sm1", 7, F32, 512)]
    ps_tr = psum("ps_tr", 4, BF16, 1024)
    ps_tr2 = psum("ps_tr2", 6, BF16, 128)

    all_sems = ("T", "V", "S", "P") + DMA_SEMS
    sems = {n: ctx.enter_context(nc.semaphore(n)) for n in all_sems}
    S = Sched()
    S.sem("T", "tensor"); S.sem("V", "vector"); S.sem("S", "scalar")
    S.sem("P", "gpsimd")
    for n in DMA_SEMS:
        S.sem(n, "gpsimd" if n in ("C", "G") else ("scalar" if n.startswith("L") else "sync"))

    S.add("sync", lambda e: e.dma_start(out=ident[:, :], in_=identd), inc="Yc", inc_by=16)
    S.add("sync", lambda e: e.dma_start(out=ones[:, :], in_=onesd), inc="Yc", inc_by=16)
    consts_tok = S.add("sync", lambda e: e.dma_start(out=bb[:, :], in_=bbd), inc="Yc", inc_by=16)

    def ln_stats_mms(rhs_val_fn, rhs_sq_fn, val_tok, sq_tok, fc, extra=()):
        out_tok = None
        for (h0, h1) in HALVES:
            w = [val_tok, consts_tok] + (list(extra) if fc == 0 else [])
            S.add("tensor",
                  (lambda e, h0=h0, h1=h1: e.matmul(
                      ps_c[:, h0:h1], ones[:, :], rhs_val_fn(h0, h1),
                      start=(fc == 0), stop=(fc == NF - 1))),
                  waits=w)
        for i, (h0, h1) in enumerate(HALVES):
            out_tok = S.add("tensor",
                            (lambda e, h0=h0, h1=h1: e.matmul(
                                ps_d[:, h0:h1], ones[:, :], rhs_sq_fn(h0, h1),
                                start=(fc == 0), stop=(fc == NF - 1))),
                            waits=[sq_tok], inc="T" if i == 1 else None)
        return out_tok

    def ln_minv(stats_tok):
        S.add("vector",
              lambda e: e.tensor_scalar_mul(mstat[:, :], ps_c[:, :], INV_D),
              waits=[stats_tok])
        S.add("vector",
              lambda e: e.tensor_mul(invstat[:, :], mstat[:, :], mstat[:, :]))
        t2 = S.add("vector",
                   lambda e: e.scalar_tensor_tensor(
                       scr[:, :], ps_d[:, :], INV_D, invstat[:, :],
                       ALU.mult, ALU.subtract),
                   inc="V")
        t3 = S.add("scalar",
                   lambda e: e.activation(invstat[:, :], scr[:, :], AF.Sqrt,
                                          bias=EPS, scale=np.float32(1.0)),
                   waits=[t2], inc="S")
        return S.add("vector",
                     lambda e: e.reciprocal(scr[:, :], invstat[:, :]),
                     waits=[t3], inc="V")

    cast_tok = [None] * NF
    p1_sq_mm = [None] * NF
    grp_cast = [None] * 8
    grp_load = [None] * 8
    for g in range(8):
        w = [grp_cast[g - 2]] if g >= 2 else []
        grp_load[g] = S.add(
            "sync",
            (lambda e, g=g: e.dma_start(
                out=xstage[g % 2][:, :].rearrange("p (s t) -> p s t", s=4),
                in_=xT[g * 512:(g + 1) * 512, :].rearrange("(s p) t -> p s t", p=128))),
            waits=w, inc=f"Yx{g % 2}", inc_by=16)
        for s in range(4):
            fc = g * 4 + s
            cast_tok[fc] = S.add(
                "vector",
                (lambda e, g=g, s=s, fc=fc: e.tensor_copy(
                    A[:, fc * 1024:(fc + 1) * 1024],
                    xstage[g % 2][:, s * 1024:(s + 1) * 1024])),
                waits=[grp_load[g]], inc="V")
            wsq = [p1_sq_mm[fc - 2]] if fc >= 2 else []
            sq_tok = S.add(
                "scalar",
                (lambda e, fc=fc: e.activation(
                    sq1[fc % 2][:, :], A[:, fc * 1024:(fc + 1) * 1024], AF.Square)),
                waits=wsq + [cast_tok[fc]], inc="S")
            p1_sq_mm[fc] = ln_stats_mms(
                (lambda h0, h1, fc=fc: A[:, fc * 1024 + h0: fc * 1024 + h1]),
                (lambda h0, h1, fc=fc: sq1[fc % 2][:, h0:h1]),
                cast_tok[fc], sq_tok, fc)
        grp_cast[g] = cast_tok[g * 4 + 3]
    minv1 = ln_minv(p1_sq_mm[NF - 1])
    norm1 = [None] * NF
    for fc in range(NF):
        S.add("vector",
              (lambda e, fc=fc: e.tensor_sub(
                  invstat[:, :], A[:, fc * 1024:(fc + 1) * 1024], mstat[:, :])),
              waits=[minv1])
        norm1[fc] = S.add(
            "vector",
            (lambda e, fc=fc: e.tensor_mul(
                A[:, fc * 1024:(fc + 1) * 1024], invstat[:, :], scr[:, :])),
            inc="V")

    prev_user = [None, None]
    prev_evt = [None, None]

    def emit_wchunk(oc, wsrc, dest, per_ic_gate=None):
        par = oc % 2
        ld = S.add(
            "gpsimd",
            (lambda e: e.dma_start(out=wbf[par][:, :], in_=wsrc[oc])),
            waits=[prev_user[par]], inc="G", inc_by=16)
        prev_user[par] = None
        mm = None
        for ic in range(NF):
            for i, (h0, h1) in enumerate(HALVES):
                w = []
                if i == 0 and per_ic_gate is not None:
                    w.append(per_ic_gate[ic])
                if ic == 0 and i == 0:
                    w.append(ld)
                    if prev_evt[par] is not None:
                        w.append(prev_evt[par])
                last = (ic == NF - 1 and i == 1)
                mm = S.add(
                    "tensor",
                    (lambda e, ic=ic, h0=h0, h1=h1: e.matmul(
                        (ps_a if par == 0 else ps_b)[:, h0:h1],
                        wbf[par][:, ic * 128:(ic + 1) * 128],
                        A[:, ic * 1024 + h0: ic * 1024 + h1],
                        start=(ic == 0), stop=(ic == NF - 1))),
                    waits=w, inc="T" if last else None)
        prev_user[par] = mm
        ev = dest(oc, (ps_a if par == 0 else ps_b), mm)
        prev_evt[par] = ev
        return mm, ev

    k_store = [None] * NF

    def k_dest(oc, pst, mm_tok):
        w = [mm_tok]
        if oc >= 2:
            w.append(k_store[oc - 2])
        ev = S.add("vector",
                   (lambda e, oc=oc, pst=pst: e.tensor_copy(kev[oc % 2][:, :], pst[:, :])),
                   waits=w, inc="V")
        k_store[oc] = S.add(
            "scalar",
            (lambda e, oc=oc: e.dma_start(
                out=ag_in2[oc // 8, (oc % 8) * 128:(oc % 8 + 1) * 128, :],
                in_=kev[oc % 2][:, :])),
            waits=[ev], inc=f"Lk{oc % 2}", inc_by=16)
        return ev

    v_store = [None] * NF
    v_tr = [None] * NF
    v_ev2 = [None] * NF

    def v_dest(oc, pst, mm_tok):
        w = [mm_tok]
        if oc >= 1:
            w.append(v_tr[oc - 1])
        if oc == 0:
            w.append(minv1)
        ev1 = S.add("vector",
                    (lambda e, pst=pst: e.tensor_copy(vev[:, :], pst[:, :])),
                    waits=w, inc="V")
        for j in range(8):
            w2 = [ev1]
            if j == 0 and oc >= 1:
                w2.append(v_ev2[oc - 1])
            v_tr[oc] = S.add(
                "tensor",
                (lambda e, j=j: e.transpose(
                    ps_tr[:, j * 128:(j + 1) * 128], vev[:, j * 128:(j + 1) * 128],
                    ident[:, :])),
                waits=w2, inc="T" if j == 7 else None)
        w3 = [v_tr[oc]]
        if oc >= 1:
            w3.append(v_store[oc - 1])
        v_ev2[oc] = S.add("vector",
                          (lambda e: e.tensor_copy(vtr[:, :], ps_tr[:, :])),
                          waits=w3, inc="V")
        v_store[oc] = S.add(
            "scalar",
            (lambda e, oc=oc: e.dma_start(
                out=ag_in2[oc // 8, 1024:2048,
                           (oc % 8) * 128:(oc % 8 + 1) * 128].rearrange(
                               "(j p) f -> p j f", p=128),
                in_=vtr[:, :].rearrange("p (j f) -> p j f", f=128))),
            waits=[v_ev2[oc]], inc="Lv", inc_by=16)
        return ev1

    def q_dest(oc, pst, mm_tok):
        return S.add("vector",
                     (lambda e, oc=oc, pst=pst: e.tensor_copy(
                         B[:, oc * 1024:(oc + 1) * 1024], pst[:, :])),
                     waits=[mm_tok], inc="V")

    ag_tok = [None] * 4
    q_evt = [None] * NF
    for sl in range(4):
        for oc in range(8 * sl, 8 * sl + 8):
            emit_wchunk(oc, WKs, k_dest, per_ic_gate=norm1 if oc == 0 else None)
        for oc in range(8 * sl, 8 * sl + 8):
            emit_wchunk(oc, WVs, v_dest)
        ag_tok[sl] = S.add(
            "gpsimd",
            (lambda e, sl=sl: e.collective_compute(
                "AllGather", ALU.bypass,
                replica_groups=[[0, 1, 2, 3], [4, 5, 6, 7]],
                ins=[ag_in2[sl].opt()], outs=[ag_out2[sl].opt()])),
            waits=[k_store[8 * sl + 6], k_store[8 * sl + 7], v_store[8 * sl + 7]],
            inc="C")
    for oc in range(NF):
        _, q_evt[oc] = emit_wchunk(oc, WQs, q_dest)
    q_done = q_evt[NF - 1]

    kt_ld = [None] * NH
    kv_ld = [None] * NH
    exp_tok = {}
    sct = {}
    pvt = {}
    aev_tok = [None] * 64
    kt_lastuse = [None] * NH
    vt_lastuse = [None] * NH
    ag_all = S.final("C")

    def emit_pvsum(u_):
        g_ = u_ // NU
        h_ = g_ // 2
        ph_ = g_ % 2
        j_ = u_ % NU
        first = (j_ == 0)
        last = (j_ == NU - 1)
        tk = None
        for cj in range(2):
            c = 2 * j_ + cj
            w = []
            if cj == 0:
                w.append(exp_tok[u_])
                if first:
                    if g_ >= 2:
                        w.append(aev_tok[g_ - 2])
                    else:
                        w.append(minv1)
                    if g_ % 2 == 0:
                        w.append(kv_ld[h_])
            S.add("tensor",
                  (lambda e, h_=h_, ph_=ph_, c=c, u_=u_, cj=cj, first=first, last=last:
                   e.matmul(pv[ph_][:, :], vt[h_ % 2][:, c * 128:(c + 1) * 128],
                            et[u_ % 2][:, cj * 512:(cj + 1) * 512],
                            start=(first and cj == 0), stop=(last and cj == 1))),
                  waits=w)
            tk = S.add("tensor",
                       (lambda e, ph_=ph_, u_=u_, cj=cj, first=first, last=last:
                        e.matmul(sm[ph_][:, :], ones[:, :],
                                 et[u_ % 2][:, cj * 512:(cj + 1) * 512],
                                 start=(first and cj == 0), stop=(last and cj == 1))),
                       inc="T" if cj == 1 else None)
        pvt[u_] = tk
        if last:
            if g_ % 2 == 1:
                vt_lastuse[h_] = tk
            q0 = (g_ % 2) * 512
            S.add("vector",
                  (lambda e, ph_=ph_: e.reciprocal(invat[:, :], sm[ph_][:, :])),
                  waits=[tk])
            aev_tok[g_] = S.add(
                "vector",
                (lambda e, h_=h_, ph_=ph_, q0=q0: e.tensor_mul(
                    A[:, h_ * 1024 + q0: h_ * 1024 + q0 + 512],
                    pv[ph_][:, :], invat[:, :])),
                inc="V")

    for h in range(NH):
        sl = h // 8
        w = [ag_tok[sl]]
        if h >= 2:
            w.append(kt_lastuse[h - 2])
        for r in range(4):
            kt_ld[h] = S.add(
                "sync",
                (lambda e, h=h, r=r: e.dma_start(
                    out=kt[h % 2][:, r * 1024:(r + 1) * 1024],
                    in_=ag_out2[h // 8, r, (h % 8) * 128:(h % 8 + 1) * 128, :])),
                waits=w if r == 0 else [], inc=f"Ykt{h % 2}", inc_by=16)
        w2 = [ag_tok[sl]]
        if h >= 2:
            w2.append(vt_lastuse[h - 2])
        for r in range(4):
            kv_ld[h] = S.add(
                "sync",
                (lambda e, h=h, r=r: e.dma_start(
                    out=vt[h % 2][:, r * 1024:(r + 1) * 1024].rearrange(
                        "p (kc f) -> p kc f", f=128),
                    in_=ag_out2[h // 8, r, 1024:2048,
                                (h % 8) * 128:(h % 8 + 1) * 128].rearrange(
                                    "(kc p) f -> p kc f", p=128))),
                waits=w2 if r == 0 else [], inc=f"Yvt{h % 2}", inc_by=16)
        for qh in range(2):
            g = h * 2 + qh
            q0 = qh * 512
            for j in range(NU):
                u = g * NU + j
                for cj in range(2):
                    c = 2 * j + cj
                    w3 = []
                    if cj == 0:
                        if u >= 2:
                            w3.append(exp_tok[u - 2])
                        if u == 0:
                            w3.append(q_done)
                        if j == 0 and qh == 0:
                            w3.append(kt_ld[h])
                    tk = S.add(
                        "tensor",
                        (lambda e, h=h, c=c, cj=cj, u=u, q0=q0: e.matmul(
                            scU[u % 2][:, cj * 512:(cj + 1) * 512],
                            kt[h % 2][:, c * 128:(c + 1) * 128],
                            B[:, h * 1024 + q0: h * 1024 + q0 + 512],
                            start=True, stop=True)),
                        waits=w3, inc="T" if cj == 1 else None)
                sct[u] = tk
                if j == NU - 1 and qh == 1:
                    kt_lastuse[h] = tk
                w4 = [sct[u]]
                if u >= 2:
                    w4.append(pvt[u - 2])
                if u < 2:
                    w4.append(ag_all)
                exp_tok[u] = S.add(
                    "scalar",
                    (lambda e, u=u: e.activation(
                        et[u % 2][:, :], scU[u % 2][:, :],
                        AF.Exp, bias=EXP_BIAS, scale=INV_SQRT_DK)),
                    waits=w4, inc="S")
                if u >= 1:
                    emit_pvsum(u - 1)
    emit_pvsum(64 * NU - 1)
    attn_done = aev_tok[63]
    last_exp = exp_tok[64 * NU - 1]

    o_ld = [None] * NF
    o_mm = [None] * NF
    o_vo = [None] * NF
    o_st = [None] * NF
    for oc in range(NF):
        o_ld[oc] = S.add(
            "gpsimd",
            (lambda e, oc=oc: e.dma_start(out=wbf[oc % 2][:, :], in_=WOs[oc])),
            waits=[prev_user[oc % 2]], inc="G", inc_by=16)
        prev_user[oc % 2] = None
        for ic in range(NF):
            for i, (h0, h1) in enumerate(HALVES):
                w = []
                if ic == 0 and i == 0:
                    w = [o_ld[oc], attn_done, last_exp]
                    if oc >= 2:
                        w.append(o_vo[oc - 2])
                last = (ic == NF - 1 and i == 1)
                o_mm[oc] = S.add(
                    "tensor",
                    (lambda e, oc=oc, ic=ic, h0=h0, h1=h1: e.matmul(
                        (ps_a if oc % 2 == 0 else ps_b)[:, h0:h1],
                        wbf[oc % 2][:, ic * 128:(ic + 1) * 128],
                        A[:, ic * 1024 + h0: ic * 1024 + h1],
                        start=(ic == 0), stop=(ic == NF - 1))),
                    waits=w, inc="T" if last else None)
        prev_user[oc % 2] = o_mm[oc]
        o_vo[oc] = S.add(
            "scalar",
            (lambda e, oc=oc: e.copy(
                B[:, oc * 1024:(oc + 1) * 1024],
                (ps_a if oc % 2 == 0 else ps_b)[:, :])),
            waits=[o_mm[oc]], inc="S")
        w = [o_vo[oc]]
        if oc >= 2:
            w.append(o_st[oc - 2])
        sq_tok = S.add(
            "scalar",
            (lambda e, oc=oc: e.activation(
                sqo[oc % 2][:, :], B[:, oc * 1024:(oc + 1) * 1024], AF.Square)),
            waits=w, inc="S")
        extra = [attn_done] if oc == 0 else []
        o_st[oc] = ln_stats_mms(
            (lambda h0, h1, oc=oc: B[:, oc * 1024 + h0: oc * 1024 + h1]),
            (lambda h0, h1, oc=oc: sqo[oc % 2][:, h0:h1]),
            o_vo[oc], sq_tok, oc, extra=extra)
    minv6 = ln_minv(o_st[NF - 1])

    qka_ld = [None] * 4
    for g4 in range(3):
        qka_ld[g4] = S.add(
            "gpsimd",
            (lambda e, g4=g4: e.dma_start(
                out=qkabf[:, g4 * 5120:(g4 + 1) * 5120],
                in_=QKAsd[:, g4 * 5120:(g4 + 1) * 5120])),
            waits=[attn_done] if g4 == 0 else [], inc="G", inc_by=16)
    bmp_ld = S.add("gpsimd",
                   (lambda e: e.dma_start(out=BmP[:, :], in_=BmPd)),
                   waits=[], inc="G", inc_by=16)

    xr_ld = [None] * NF
    x2_tok = [None] * NF
    for oc in range(NF):
        if oc < 4:
            w = [o_mm[30 + (oc // 2)]]
        else:
            w = [x2_tok[oc - 4]]
        xr_ld[oc] = S.add(
            "sync",
            (lambda e, oc=oc: e.dma_start(
                out=xr[oc % 4][:, :], in_=xT[oc * 128:(oc + 1) * 128, :])),
            waits=w, inc=f"Yxr{oc % 4}", inc_by=16)
        S.add("vector",
              (lambda e, oc=oc: e.tensor_sub(
                  invstat[:, :], B[:, oc * 1024:(oc + 1) * 1024], mstat[:, :])),
              waits=[minv6])
        S.add("vector",
              (lambda e: e.tensor_mul(sqo[0][:, :], invstat[:, :], scr[:, :])))
        x2_tok[oc] = S.add(
            "vector",
            (lambda e, oc=oc: e.tensor_add(
                B[:, oc * 1024:(oc + 1) * 1024], sqo[0][:, :], xr[oc % 4][:, :])),
            waits=[xr_ld[oc]], inc="V")
    x2_done = x2_tok[NF - 1]

    p7_mm = [None] * NF
    for oc in range(NF):
        w = [x2_tok[oc]]
        if oc >= 2:
            w.append(p7_mm[oc - 2])
        sq_tok = S.add(
            "scalar",
            (lambda e, oc=oc: e.activation(
                sqo[oc % 2][:, :], B[:, oc * 1024:(oc + 1) * 1024], AF.Square)),
            waits=w, inc="S")
        p7_mm[oc] = ln_stats_mms(
            (lambda h0, h1, oc=oc: B[:, oc * 1024 + h0: oc * 1024 + h1]),
            (lambda h0, h1, oc=oc: sqo[oc % 2][:, h0:h1]),
            x2_tok[oc], sq_tok, oc, extra=[minv6])
    minv7 = ln_minv(p7_mm[NF - 1])
    ln2_tok = [None] * NF
    for oc in range(NF):
        S.add("vector",
              (lambda e, oc=oc: e.tensor_sub(
                  invstat[:, :], B[:, oc * 1024:(oc + 1) * 1024], mstat[:, :])),
              waits=[minv7, o_mm[NF - 1]])
        ln2_tok[oc] = S.add(
            "vector",
            (lambda e, oc=oc: e.tensor_mul(
                A[:, oc * 1024:(oc + 1) * 1024], invstat[:, :], scr[:, :])),
            inc="V")

    qka_ld[3] = S.add(
        "gpsimd",
        (lambda e: e.dma_start(
            out=qkabf[:, 3 * 5120:4 * 5120],
            in_=QKAsd[:, 3 * 5120:4 * 5120])),
        waits=[p7_mm[NF - 1]], inc="G", inc_by=16)

    qkh_mm = [None] * 8
    qkh_ev = [None] * 8
    scrmul = [None] * 8
    gt_ev = [None] * 8
    for tc in range(8):
        pq = ps_a if tc % 2 == 0 else ps_b
        hofs = 0 if tc % 2 == 0 else 512
        for ic in range(NF):
            lic, g4 = ic % 8, ic // 8
            w = [qka_ld[g4]] + ([ln2_tok[ic]] if tc == 0 else [])
            if ic == 0 and tc >= 2:
                w += [scrmul[tc - 2], qkh_ev[tc - 2]]
            if ic == 0 and tc < 2:
                w.append(o_mm[NF - 1])
            S.add("tensor",
                  (lambda e, tc=tc, ic=ic, g4=g4, lic=lic, pq=pq: e.matmul(
                      pq[:, 0:256],
                      A[:, ic * 1024 + tc * 128: ic * 1024 + (tc + 1) * 128],
                      qkabf[:, g4 * 5120 + lic * 640: g4 * 5120 + lic * 640 + 256],
                      start=(ic == 0), stop=(ic == NF - 1))),
                  waits=w)
            S.add("tensor",
                  (lambda e, tc=tc, ic=ic, g4=g4, lic=lic, pq=pq: e.matmul(
                      pq[:, 512:768],
                      A[:, ic * 1024 + tc * 128: ic * 1024 + (tc + 1) * 128],
                      qkabf[:, g4 * 5120 + lic * 640 + 256: g4 * 5120 + lic * 640 + 512],
                      start=(ic == 0), stop=(ic == NF - 1))))
            qkh_mm[tc] = S.add(
                "tensor",
                (lambda e, tc=tc, ic=ic, g4=g4, lic=lic, hofs=hofs: e.matmul(
                    ps_c[:, hofs: hofs + 128],
                    A[:, ic * 1024 + tc * 128: ic * 1024 + (tc + 1) * 128],
                    qkabf[:, g4 * 5120 + lic * 640 + 512: g4 * 5120 + lic * 640 + 640],
                    start=(ic == 0), stop=(ic == NF - 1))),
                inc="T" if ic == NF - 1 else None)
        kcp = S.add("scalar",
                    (lambda e, tc=tc, pq=pq: e.copy(
                        kbuf[:, 0:256] if tc % 2 == 0 else kbuf[:, 1024:1280],
                        pq[:, 512:768])),
                    waits=[qkh_mm[tc]], inc="S")
        qkh_ev[tc] = S.add(
            "scalar",
            (lambda e, tc=tc, hofs=hofs: e.copy(
                hbuf[:, tc % 2 * 512: tc % 2 * 512 + 128], ps_c[:, hofs: hofs + 128])),
            inc="S")
        scrmul[tc] = S.add(
            "vector",
            (lambda e, tc=tc, pq=pq: e.tensor_mul(
                scr[:, 0:256], pq[:, 0:256],
                kbuf[:, 0:256] if tc % 2 == 0 else kbuf[:, 1024:1280])),
            waits=[kcp], inc="V")
        S.add("vector",
              (lambda e: e.reduce_sum(
                  sc8[:, 0:8],
                  scr[:, 0:256].rearrange("p (ex k) -> p ex k", ex=8), axis=AX.X)))
        S.add("vector", lambda e: e.tensor_scalar_mul(w8[:, :], sc8[:, :], INV_SQRT_DK_FFN))
        S.add("vector", lambda e: e.reduce_max(smax[:, 0:1], w8[:, :], axis=AX.X))
        tse = S.add("vector",
                    lambda e: e.tensor_scalar_sub(sc8[:, :], w8[:, :], smax[:, 0:1]),
                    inc="V")
        texp = S.add("scalar",
                     (lambda e: e.activation(w8[:, :], sc8[:, :], AF.Exp)),
                     waits=[tse], inc="S")
        S.add("vector", lambda e: e.reduce_sum(ssum[:, 0:1], w8[:, :], axis=AX.X),
              waits=[texp])
        S.add("vector", lambda e: e.reciprocal(srec[:, 0:1], ssum[:, 0:1]))
        S.add("vector", lambda e: e.tensor_scalar_mul(sc8[:, :], w8[:, :], srec[:, 0:1]))
        for ex in range(8):
            S.add("vector",
                  (lambda e, ex=ex: e.tensor_scalar_mul(
                      scr[:, 512 + ex * 16: 512 + (ex + 1) * 16],
                      ones[:, 0:16], sc8[:, ex:ex + 1])))
        gv = S.add("vector",
                   (lambda e, tc=tc: e.tensor_mul(
                       gtile[:, :], hbuf[:, tc % 2 * 512: tc % 2 * 512 + 128],
                       scr[:, 512:640])),
                   waits=[qkh_ev[tc]], inc="V")
        w = [gv]
        if tc >= 1:
            w.append(gt_ev[tc - 1])
        if tc == 0:
            w.append(attn_done)
        gtr = S.add("tensor",
                    (lambda e: e.transpose(ps_tr2[:, :], gtile[:, :], ident[:, :])),
                    waits=w, inc="T")
        gt_ev[tc] = S.add("scalar",
                          (lambda e, tc=tc: e.copy(
                              gt[:, tc * 128:(tc + 1) * 128], ps_tr2[:, :])),
                          waits=[gtr], inc="S")
    gt_done = gt_ev[7]

    st8 = {}
    d_vo_prev = {}
    vout_prev = {}
    for hf in range(2):
        t0 = hf * 512
        d_ld = [None] * NF
        d_mm = [None] * NF
        d_vo = [None] * NF
        d_st = [None] * NF
        sqv_tok = [None] * NF
        vout_cur = {}

        def emit_stats8(j, d_vo=d_vo, d_st=d_st, sqv_tok=sqv_tok):
            S.add("tensor",
                  (lambda e, j=j: e.matmul(
                      ps_b[:, 0:512], ones[:, :],
                      vhalf[:, j * 512:(j + 1) * 512],
                      start=(j == 0), stop=(j == NF - 1))),
                  waits=[d_vo[j]])
            d_st[j] = S.add(
                "tensor",
                (lambda e, j=j: e.matmul(
                    ps_b[:, 512:1024], ones[:, :], sq8[:, :],
                    start=(j == 0), stop=(j == NF - 1))),
                waits=[sqv_tok[j]], inc="T")

        for oc in range(NF):
            w = [prev_user[oc % 2]]
            if hf == 0 and oc < 2:
                w.append(x2_tok[29 if oc == 0 else 31])
            d_ld[oc] = S.add(
                "gpsimd",
                (lambda e, oc=oc: e.dma_start(out=wbf[oc % 2][:, :], in_=Ws[oc])),
                waits=w, inc="G", inc_by=16)
            prev_user[oc % 2] = None
            po = 0 if oc % 2 == 0 else 512
            w = [bmp_ld, gt_done]
            if oc >= 2:
                w.append(d_vo[oc - 2])
            elif hf == 1:
                w.append(d_vo_prev[NF - 2 + oc])
            S.add("tensor",
                  (lambda e, oc=oc, po=po, t0=t0: e.matmul(
                      ps_a[:, po: po + 512],
                      BmP[:, oc * 128:(oc + 1) * 128],
                      gt[:, t0:t0 + 512],
                      start=True, stop=False)),
                  waits=w)
            for ic in range(NF):
                w = [d_ld[oc]] if ic == 0 else []
                last = (ic == NF - 1)
                d_mm[oc] = S.add(
                    "tensor",
                    (lambda e, oc=oc, ic=ic, po=po, t0=t0: e.matmul(
                        ps_a[:, po: po + 512],
                        wbf[oc % 2][:, ic * 128:(ic + 1) * 128],
                        A[:, ic * 1024 + t0: ic * 1024 + t0 + 512],
                        start=False, stop=last)),
                    waits=w, inc="T" if last else None)
            prev_user[oc % 2] = d_mm[oc]
            if oc >= 1:
                emit_stats8(oc - 1)
            w = [d_mm[oc]]
            if hf == 1:
                w.append(vout_prev[oc])
            d_vo[oc] = S.add(
                "scalar",
                (lambda e, oc=oc, po=po: e.add(
                    vhalf[:, oc * 512:(oc + 1) * 512],
                    ps_a[:, po: po + 512], bb[:, oc:oc + 1])),
                waits=w, inc="S")
            w = [d_vo[oc]]
            if oc >= 1:
                w.append(d_st[oc - 1])
            sqv_tok[oc] = S.add(
                "scalar",
                (lambda e, oc=oc: e.activation(
                    sq8[:, :], vhalf[:, oc * 512:(oc + 1) * 512], AF.Square)),
                waits=w, inc="S")
        emit_stats8(NF - 1)
        d_vo_prev = {i: d_vo[i] for i in range(NF)}
        w = [d_st[NF - 1]]
        if hf == 1:
            w += [st8[(0, NF - 2)], st8[(0, NF - 1)]]
        S.add("vector",
              (lambda e: e.tensor_scalar_mul(mstat[:, 0:512], ps_b[:, 0:512], INV_D)),
              waits=w)
        S.add("vector",
              (lambda e: e.tensor_mul(invstat[:, 0:512], mstat[:, 0:512], mstat[:, 0:512])))
        t2 = S.add("vector",
                   (lambda e: e.scalar_tensor_tensor(
                       scr[:, 0:512], ps_b[:, 512:1024], INV_D, invstat[:, 0:512],
                       ALU.mult, ALU.subtract)),
                   inc="V")
        t3 = S.add("scalar",
                   (lambda e: e.activation(invstat[:, 0:512], scr[:, 0:512], AF.Sqrt,
                                           bias=EPS, scale=np.float32(1.0))),
                   waits=[t2], inc="S")
        minv8 = S.add("vector",
                      (lambda e: e.reciprocal(mstat[:, 512:1024], invstat[:, 0:512])),
                      waits=[t3], inc="V")
        for oc in range(NF):
            w = [minv8]
            if oc >= 2:
                w.append(st8[(hf, oc - 2)])
            oofs = 0 if oc % 2 == 0 else 512
            if hf == 1 and oc % 2 == 1:
                S.add("gpsimd",
                      (lambda e, oc=oc: e.tensor_sub(
                          pt0[:, 0:512], vhalf[:, oc * 512:(oc + 1) * 512],
                          mstat[:, 0:512])),
                      waits=w)
                S.add("gpsimd",
                      (lambda e: e.tensor_mul(pt0[:, 512:1024], pt0[:, 0:512],
                                              mstat[:, 512:1024])))
                vout = S.add(
                    "gpsimd",
                    (lambda e, oc=oc, oofs=oofs, t0=t0: e.tensor_add(
                        invstat[:, oofs:oofs + 512], pt0[:, 512:1024],
                        B[:, oc * 1024 + t0: oc * 1024 + t0 + 512])),
                    inc="P")
            else:
                S.add("vector",
                      (lambda e, oc=oc: e.tensor_sub(
                          scr[:, 0:512], vhalf[:, oc * 512:(oc + 1) * 512],
                          mstat[:, 0:512])),
                      waits=w)
                S.add("vector",
                      (lambda e: e.tensor_mul(scr[:, 512:1024], scr[:, 0:512],
                                              mstat[:, 512:1024])))
                vout = S.add(
                    "vector",
                    (lambda e, oc=oc, oofs=oofs, t0=t0: e.tensor_add(
                        invstat[:, oofs:oofs + 512], scr[:, 512:1024],
                        B[:, oc * 1024 + t0: oc * 1024 + t0 + 512])),
                    inc="V")
            vout_cur[oc] = vout
            st8[(hf, oc)] = S.add(
                "scalar",
                (lambda e, oc=oc, oofs=oofs, t0=t0: e.dma_start(
                    out=outT[oc * 128:(oc + 1) * 128, t0:t0 + 512],
                    in_=invstat[:, oofs:oofs + 512])),
                waits=[vout], inc=f"Lo{oc % 2}", inc_by=16)
        vout_prev = vout_cur
    S.add("scalar", None, waits=[S.final("Lo0"), S.final("Lo1")])

    for ename in S.ENGINES:
        pruned = []
        for emit, waits, inc, inc_by in S.ops[ename]:
            w2 = [(snm, v) for (snm, v) in waits
                  if not (S.producer[snm] == ename and snm not in DMA_SEMS)]
            pruned.append((emit, w2, inc, inc_by))
        S.ops[ename] = pruned

    def emitter(ops, self_chain_sem=None):
        def run(eng):
            last = {}
            n_self = 0
            for emit, waits, inc, inc_by in ops:
                for (snm, val) in waits:
                    if last.get(snm, -1) >= val:
                        continue
                    eng.wait_ge(sems[snm], val)
                    last[snm] = val
                if emit is None:
                    continue
                if self_chain_sem is not None and n_self >= 1:
                    eng.wait_ge(sems[self_chain_sem], n_self)
                inst = emit(eng)
                if inc is not None:
                    inst.then_inc(sems[inc], inc_by)
                    if inc == self_chain_sem:
                        n_self += inc_by
        return run

    with nc.Block() as block:
        block.sync(emitter(S.ops["sync"]))
        block.gpsimd(emitter(S.ops["gpsimd"]))
        block.scalar(emitter(S.ops["scalar"]))
        block.vector(emitter(S.ops["vector"], self_chain_sem="V"))
        block.tensor(emitter(S.ops["tensor"]))

    nc._keepalive = ctx
    return nc


def _chunk(wT):
    return np.ascontiguousarray(
        wT.reshape(32, 128, 32, 128).transpose(2, 1, 0, 3).reshape(32, 128, 4096))


def _host_stage(inputs):
    bf = ml_dtypes.bfloat16
    x = np.asarray(inputs["L0_attn_x"], dtype=np.float32)
    wqka = np.concatenate([
        np.asarray(inputs["L0_ffn_Wq"], np.float32),
        np.asarray(inputs["L0_ffn_Wk"], np.float32),
        np.asarray(inputs["L0_ffn_A"], np.float32).reshape(128, 4096)], axis=0).T
    stage = {
        "WQs": _chunk(np.asarray(inputs["L0_attn_w_q"], np.float32).T),
        "WKs": _chunk(np.asarray(inputs["L0_attn_w_k"], np.float32).T),
        "WVs": _chunk(np.asarray(inputs["L0_attn_w_v"], np.float32).T),
        "WOs": _chunk(np.asarray(inputs["L0_attn_w_o"], np.float32).T),
        "Ws": _chunk(np.asarray(inputs["L0_ffn_W"], np.float32).T),
        "QKAsd": np.ascontiguousarray(
            wqka.reshape(4, 8, 128, 640).transpose(2, 0, 1, 3).reshape(128, 20480)),
        "BmPd": np.ascontiguousarray(
            np.asarray(inputs["L0_ffn_Bm"], np.float32).transpose(0, 2, 1).reshape(128, 4096)),
        "bbd": np.ascontiguousarray(np.asarray(inputs["L0_ffn_b"], np.float32).reshape(32, 128).T),
        "identd": np.eye(128, dtype=bf),
        "onesd": np.ones((128, 128), dtype=bf),
    }
    in_maps = []
    for c in range(8):
        b, q = c // 4, c % 4
        m = dict(stage)
        m["xT"] = np.ascontiguousarray(x[b, q * 1024:(q + 1) * 1024, :].T)
        in_maps.append(m)
    return in_maps


def kernel(**inputs):
    in_maps = _host_stage(inputs)
    nc = build_nc()
    res = run_bass_kernel_spmd(nc, in_maps, core_ids=list(range(8)))
    out = np.empty((2, 4096, 4096), dtype=np.float32)
    for c in range(8):
        b, q = c // 4, c % 4
        out[b, q * 1024:(q + 1) * 1024, :] = res.results[c]["outT"].T
    return out
